# version 01
# baseline (speedup 1.0000x reference)
import contextlib
import numpy as np
import ml_dtypes
import concourse.bass as bass
import concourse.mybir as mybir
from concourse.bass_utils import run_bass_kernel_spmd
from concourse.bass_types import AP as _AP


def _ap(a):
    if a is not None and not isinstance(a, _AP) and hasattr(a, "ap"):
        return a.ap()
    return a

F32 = mybir.dt.float32
BF16 = mybir.dt.bfloat16
AF = mybir.ActivationFunctionType
ALU = mybir.AluOpType
AX = mybir.AxisListType
RND = 12582912.0
BF = ml_dtypes.bfloat16

S, D = 8192, 2048
NCORES = 8
TS = S // NCORES
NT = TS // 128
NTF = S // 128

NH = 16
HPC = NH // NCORES
Q_LORA, KV_LORA = 1536, 512
NOPE, ROPE_D, VH = 128, 64, 128
QH = NOPE + ROPE_D
SCALE = QH ** -0.5

BK, BV, DK, DV, KCONV = 16, 32, 128, 128, 4
KPC = BK // NCORES
VPC = BV // NCORES
CCH = KPC * DK * 2 + VPC * DV
QKVZ_COLS = CCH + VPC * DV
CHUNK = 128
NCHUNK = S // CHUNK


class B:

    ENGS = ("sync", "tensor", "vector", "scalar", "gpsimd")

    def __init__(self, nc):
        self.nc = nc
        self.seq = []
        self.k = 0

    def op(self, eng, method, *args, inc=1, **kwargs):
        args = tuple(_ap(a) for a in args)
        kwargs = {k_: (_ap(v) if k_ in ("out", "in_", "bias", "scale",
                                        "accum_out") else v)
                  for k_, v in kwargs.items()}
        self.seq.append((eng, self.k, method, args, kwargs, inc))
        self.k += inc

    def dma(self, out, in_, eng="sync", slow_ok=False):
        if slow_ok:
            self.op(eng, "dma_start", inc=16, out=out, in_=in_,
                    allow_slow_non_contiguous=True)
        else:
            self.op(eng, "dma_start", inc=16, out=out, in_=in_)

    def dmat(self, out, in_, eng="sync"):
        self.op(eng, "dma_start_transpose", out, in_, inc=16)

    def ts(self, out, in0, s1, s2=None, op0=ALU.mult, op1=ALU.bypass,
           eng="vector"):
        if s2 is None:
            self.op(eng, "tensor_scalar", out, in0, s1, None, op0=op0)
        else:
            self.op(eng, "tensor_scalar", out, in0, s1, s2, op0=op0, op1=op1)

    def tt(self, out, in0, in1, alu, eng="vector"):
        self.op(eng, "tensor_tensor", out, in0, in1, op=alu)

    def stt(self, out, in0, scalar, in1, op0, op1, eng="vector"):
        self.op(eng, "scalar_tensor_tensor", out, in0, scalar, in1,
                op0=op0, op1=op1)

    def cp(self, out, in_, eng="vector"):
        self.op(eng, "tensor_copy", out, in_)

    def act(self, out, in_, func, bias=0.0, scale=1.0, accum_out=None,
            eng="scalar"):
        kw = dict(bias=bias, scale=scale)
        if accum_out is not None:
            kw["accum_out"] = accum_out
        self.op(eng, "activation", out, in_, func, **kw)

    def mm(self, psum, lhsT, rhs, start, stop, eng="tensor"):
        self.op(eng, "matmul", psum, lhsT, rhs, start=start, stop=stop)

    def mm_acc(self, psum, ops):
        n = len(ops)
        for i, (l, r) in enumerate(ops):
            self.mm(psum, l, r, i == 0, i == n - 1)

    def tp(self, psum128, src, dst, ident, evac_eng="vector"):
        self.op("tensor", "transpose", psum128, src, ident)
        self.cp(dst, psum128, eng=evac_eng)

    def bcast_rows(self, dst, src_row, nrows=128, eng="gpsimd"):
        self.dma(dst[0:1, :], src_row[0:1, :], eng=eng)
        n = 1
        while n < nrows:
            m = min(n, nrows - n)
            self.dma(dst[n:n + m, :], dst[0:m, :], eng=eng)
            n += m

    def emit(self, block, sem):
        per = {e: [] for e in self.ENGS}
        for (eng, wk, meth, args, kw, inc) in self.seq:
            per[eng].append((wk, meth, args, kw, inc))
        for name in self.ENGS:
            ops = per[name]
            if not ops and name != "sync":
                continue
            dec = getattr(block, name)

            def body(e, ops=ops, name=name):
                for (wk, meth, args, kw, inc) in ops:
                    if wk:
                        e.wait_ge(sem, wk)
                    getattr(e, meth)(*args, **kw).then_inc(sem, inc)
                if name == "sync":
                    e.wait_ge(sem, self.k)
            dec(body)


def emit_identity(b, ID):
    b.op("vector", "memset", ID[:], 1.0)
    b.op("gpsimd", "affine_select", ID[:], ID[:], pattern=[[-1, 128]],
         compare_op=ALU.is_ge, fill=0.0, base=0, channel_multiplier=1)
    b.op("gpsimd", "affine_select", ID[:], ID[:], pattern=[[1, 128]],
         compare_op=ALU.is_ge, fill=0.0, base=0, channel_multiplier=-1)


def emit_ln_tm(b, x_ap, out_ap, cols, D_, eps, sub_mean=True, sq_scratch=None,
               out_dtype_cast=None):
    s1, s2, mcol, rstd = cols[:, 0:1], cols[:, 1:2], cols[:, 2:3], cols[:, 3:4]
    sq = sq_scratch if sq_scratch is not None else out_ap
    b.act(sq, x_ap, AF.Square, accum_out=s2)
    if sub_mean:
        b.op("vector", "reduce_sum", s1, x_ap, axis=AX.X)
        b.ts(mcol, s1, 1.0 / D_)
        b.ts(s2, s2, 1.0 / D_)
        b.tt(s1, mcol, mcol, ALU.mult)
        b.tt(s2, s2, s1, ALU.subtract)
    else:
        b.ts(s2, s2, 1.0 / D_)
    b.ts(s2, s2, eps, op0=ALU.add)
    b.act(s1, s2, AF.Sqrt)
    b.op("vector", "reciprocal", rstd, s1)
    if sub_mean:
        b.ts(out_ap, x_ap, mcol, rstd, op0=ALU.subtract, op1=ALU.mult)
    else:
        b.ts(out_ap, x_ap, rstd, op0=ALU.mult)


def _io(nc, name, shape, dt, kind):
    return nc.dram_tensor(name, shape, dt, kind=kind).ap()


def build_L2(first=True):
    KD = D if first else 2 * D
    KC = KD // 128
    NHALF = KD // D
    nc = bass.Bass("TRN2", target_bir_lowering=False, debug=False)
    x_in = _io(nc, "x_in", [TS, D], F32, "ExternalInput")
    xa_in = _io(nc, "xa", [TS, KD], F32 if first else BF16, "ExternalInput")
    wT_in = _io(nc, "wT", [KD, D], F32, "ExternalInput")
    fw_in = _io(nc, "fw", [D, D], F32, "ExternalInput")
    fwT_in = _io(nc, "fwT", [D, D], F32, "ExternalInput")
    fb_in = _io(nc, "fb", [1, D], F32, "ExternalInput")
    x_out = _io(nc, "x_out", [TS, D], F32, "ExternalOutput")
    ln_out = _io(nc, "ln_out", [TS, D], BF16, "ExternalOutput") if first else None

    with contextlib.ExitStack() as ctx:
        E = ctx.enter_context
        XT = E(nc.sbuf_tensor("XT", [128, D], F32))
        XAT = E(nc.sbuf_tensor("XAT", [128, D], BF16))
        XA = E(nc.sbuf_tensor("XA", [128, KD], BF16))
        WT = E(nc.sbuf_tensor("WT", [128, 16 * D], BF16))
        FWT = E(nc.sbuf_tensor("FWT", [128, 16 * D], BF16))
        GB = E(nc.sbuf_tensor("GB", [128, D], F32))
        BB = E(nc.sbuf_tensor("BB", [128, D], F32))
        ROW = E(nc.sbuf_tensor("ROW", [1, D], F32))
        TMP = E(nc.sbuf_tensor("TMP", [128, D], F32))
        TMP2 = E(nc.sbuf_tensor("TMP2", [128, D], F32))
        XQ = E(nc.sbuf_tensor("XQ", [128, D], F32))
        XQT = E(nc.sbuf_tensor("XQT", [128, D], BF16))
        COLS = E(nc.sbuf_tensor("COLS", [128, 8], F32))
        GCOL = E(nc.sbuf_tensor("GCOL", [128, 16], F32))
        ID = E(nc.sbuf_tensor("ID", [128, 128], BF16))
        IDF = E(nc.sbuf_tensor("IDF", [128, 128], F32))
        PS0 = E(nc.psum_tensor("PS0", [128, 512], F32))
        PSA = E(nc.psum_tensor("PSA", [128, 512], F32))
        PSB = E(nc.psum_tensor("PSB", [128, 512], F32))
        PSC = E(nc.psum_tensor("PSC", [128, 512], F32))
        PSD = E(nc.psum_tensor("PSD", [128, 512], F32))
        PS1 = E(nc.psum_tensor("PS1", [128, 512], F32))
        PS1B = E(nc.psum_tensor("PS1B", [128, 512], BF16))
        sem = E(nc.semaphore("chain"))
        block = E(nc.Block())
        PSN = [PSA, PSB, PSC, PSD]
        b = B(nc)
        b.dma(ROW[0:1, :], fb_in[0:1, :], eng="gpsimd")
        emit_identity(b, ID)
        emit_identity(b, IDF)
        b.bcast_rows(BB, ROW)

        for i in range(16):
            b.dma(TMP[:, :], fw_in[i * 128:(i + 1) * 128, :])
            b.op("vector", "reduce_sum", GCOL[:, i:i + 1], TMP[:], axis=AX.X,
                 apply_absolute_value=True)
        b.ts(GCOL[:], GCOL[:], 1.0 / D)
        for i in range(16):
            b.dma(ROW[0:1, i * 128:(i + 1) * 128], GCOL[:, i:i + 1],
                  eng="gpsimd")
        b.bcast_rows(GB, ROW)
        b.ts(TMP, GB, 1e-8, op0=ALU.add)
        b.op("vector", "reciprocal", TMP, TMP)
        for i in range(16):
            b.dma(TMP2[:, :], fwT_in[i * 128:(i + 1) * 128, :])
            b.tt(TMP2[:], TMP2[:], TMP[:], ALU.mult)
            b.ts(TMP2[:], TMP2[:], RND, RND, op0=ALU.add, op1=ALU.subtract)
            b.ts(TMP2[:], TMP2[:], 1.0, -1.0, op0=ALU.min, op1=ALU.max)
            b.cp(FWT[:, i * D:(i + 1) * D], TMP2[:])

        for i in range(NT):
            xt = XT[:, :]
            b.dma(xt, x_in[i * 128:(i + 1) * 128, :])
            for h in range(NHALF):
                b.dma(XAT[:, :], xa_in[i * 128:(i + 1) * 128, h * D:(h + 1) * D],
                      eng="gpsimd" if first else "sync")
                for kc in range(16):
                    b.tp(PS1B[:, 0:128], XAT[:, kc * 128:(kc + 1) * 128],
                         XA[:, (h * 16 + kc) * 128:(h * 16 + kc + 1) * 128],
                         ID[:])
            for h in range(NHALF):
                for kc in range(16):
                    b.dma(WT[:, kc * D:(kc + 1) * D],
                          wT_in[(h * 16 + kc) * 128:(h * 16 + kc + 1) * 128, :],
                          eng="gpsimd")
                for nb in range(D // 512):
                    for kc in range(16):
                        b.mm(PSN[nb][:, 0:512],
                             XA[:, (h * 16 + kc) * 128:(h * 16 + kc + 1) * 128],
                             WT[:, kc * D + nb * 512: kc * D + (nb + 1) * 512],
                             start=(h == 0 and kc == 0),
                             stop=(h == NHALF - 1 and kc == 15))
            for nb in range(D // 512):
                sl = slice(nb * 512, (nb + 1) * 512)
                b.tt(xt[:, sl], xt[:, sl], PSN[nb][:, 0:512], ALU.add)

            emit_ln_tm(b, xt, TMP2, COLS, D, 1e-5, sub_mean=True)
            emit_ln_tm(b, TMP2[:, 0:D], TMP, COLS, D, 1e-8, sub_mean=False)
            b.op("vector", "reduce_max", COLS[:, 4:5], TMP, axis=AX.X,
                 apply_absolute_value=True)
            b.ts(COLS[:, 4:5], COLS[:, 4:5], 1.0 / 127.0)
            b.ts(COLS[:, 5:6], COLS[:, 4:5], 1e-8, op0=ALU.add)
            b.op("vector", "reciprocal", COLS[:, 5:6], COLS[:, 5:6])
            b.ts(XQ, TMP, COLS[:, 5:6], RND, op0=ALU.mult, op1=ALU.add)
            b.ts(XQ, XQ, RND, 127.0, op0=ALU.subtract, op1=ALU.min)
            b.ts(XQ, XQ, -127.0, op0=ALU.max)
            for kc in range(16):
                b.tp(PS1[:, 0:128], XQ[:, kc * 128:(kc + 1) * 128],
                     XQT[:, kc * 128:(kc + 1) * 128], IDF[:])
            for nb in range(D // 512):
                b.mm_acc(PS0[:, 0:512],
                         [(XQT[:, kc * 128:(kc + 1) * 128],
                           FWT[:, kc * D + nb * 512: kc * D + (nb + 1) * 512])
                          for kc in range(16)])
                sl = slice(nb * 512, (nb + 1) * 512)
                b.stt(TMP[:, sl], PS0[:, 0:512], COLS[:, 4:5], GB[:, sl],
                      op0=ALU.mult, op1=ALU.mult)
                b.tt(TMP[:, sl], TMP[:, sl], BB[:, sl], ALU.add)
                b.tt(xt[:, sl], xt[:, sl], TMP[:, sl], ALU.add)
            b.dma(x_out[i * 128:(i + 1) * 128, :], xt)
            if first:
                emit_ln_tm(b, xt, XQT[:, 0:D], COLS, D, 1e-5, sub_mean=True,
                           sq_scratch=TMP)
                b.dma(ln_out[i * 128:(i + 1) * 128, :], XQT[:, 0:D])

        b.emit(block, sem)
    return nc


PI_HI = float(np.float32(6.2831855))
PI_LO = float(np.float64(2 * np.pi) - np.float64(np.float32(6.2831855)))
INV2PI = float(np.float32(1.0 / (2 * np.pi)))
PI_CLAMP = 3.14159255


def emit_rope_tables(b, POSC, IVB, F, K1, COS, SIN, TMPA):
    b.ts(F, IVB, POSC, op0=ALU.mult)
    for (dst, phase) in ((SIN, 0.0), (COS, np.pi / 2)):
        if phase:
            b.ts(TMPA, F, float(np.float32(phase)), op0=ALU.add)
            src = TMPA
        else:
            src = F
        b.ts(K1, src, INV2PI, RND, op0=ALU.mult, op1=ALU.add)
        b.ts(K1, K1, RND, op0=ALU.subtract)
        b.stt(dst, K1, -PI_HI, src, op0=ALU.mult, op1=ALU.add)
        b.stt(dst, K1, -PI_LO, dst, op0=ALU.mult, op1=ALU.add)
        b.ts(dst, dst, PI_CLAMP, -PI_CLAMP, op0=ALU.min, op1=ALU.max)
        b.act(dst, dst, AF.Sin)


def emit_rope_apply(b, x1, x2, COS, SIN, T1, T2, T3, T4):
    b.tt(T1, x1, COS, ALU.mult)
    b.tt(T2, x2, SIN, ALU.mult)
    b.tt(T3, x2, COS, ALU.mult)
    b.tt(T4, x1, SIN, ALU.mult)
    b.tt(x1, T1, T2, ALU.subtract)
    b.tt(x2, T3, T4, ALU.add)


def build_L1a():
    nc = bass.Bass("TRN2", target_bir_lowering=False, debug=False)
    x_in = _io(nc, "x_in", [TS, D], F32, "ExternalInput")
    pos_in = _io(nc, "pos", [TS, 1], F32, "ExternalInput")
    invf_in = _io(nc, "invf", [1, 32], F32, "ExternalInput")
    wqaT_in = _io(nc, "wqaT", [D, Q_LORA], F32, "ExternalInput")
    qlnw_in = _io(nc, "qlnw", [1, Q_LORA], F32, "ExternalInput")
    wqbT_in = _io(nc, "wqbT", [Q_LORA, NH * QH], F32, "ExternalInput")
    wkvaT_in = _io(nc, "wkvaT", [D, KV_LORA + ROPE_D], F32, "ExternalInput")
    kvlnw_in = _io(nc, "kvlnw", [1, KV_LORA], F32, "ExternalInput")
    q_out = _io(nc, "q_out", [TS, NH * QH], BF16, "ExternalOutput")
    kvn_out = _io(nc, "kvn_out", [TS, KV_LORA + ROPE_D], BF16, "ExternalOutput")

    with contextlib.ExitStack() as ctx:
        E = ctx.enter_context
        XT = E(nc.sbuf_tensor("XT", [128, D], F32))
        LN1B = E(nc.sbuf_tensor("LN1B", [128, D], BF16))
        LN1F = E(nc.sbuf_tensor("LN1F", [128, D], BF16))
        WQA = E(nc.sbuf_tensor("WQA", [128, 16 * Q_LORA], BF16))
        WQB = E(nc.sbuf_tensor("WQB", [128, 12 * NH * QH], BF16))
        WKVA = E(nc.sbuf_tensor("WKVA", [128, 16 * 576], BF16))
        QL = E(nc.sbuf_tensor("QL", [128, Q_LORA], F32))
        QLB = E(nc.sbuf_tensor("QLB", [128, Q_LORA], BF16))
        QLF = E(nc.sbuf_tensor("QLF", [128, Q_LORA], BF16))
        QA = E(nc.sbuf_tensor("QA", [128, NH * QH], F32))
        QAB = E(nc.sbuf_tensor("QAB", [128, NH * QH], BF16))
        KV = E(nc.sbuf_tensor("KV", [128, 576], F32))
        KVB = E(nc.sbuf_tensor("KVB", [128, 576], BF16))
        QLNW = E(nc.sbuf_tensor("QLNW", [128, Q_LORA], F32))
        KVLNW = E(nc.sbuf_tensor("KVLNW", [128, KV_LORA], F32))
        TMP = E(nc.sbuf_tensor("TMP", [128, D], F32))
        COLS = E(nc.sbuf_tensor("COLS", [128, 8], F32))
        POSC = E(nc.sbuf_tensor("POSC", [128, 1], F32))
        IVB = E(nc.sbuf_tensor("IVB", [128, 32], F32))
        RSC = E(nc.sbuf_tensor("RSC", [128, 8 * 32], F32))
        ID = E(nc.sbuf_tensor("ID", [128, 128], BF16))
        IDF = E(nc.sbuf_tensor("IDF", [128, 128], F32))
        PS0 = E(nc.psum_tensor("PS0", [128, 512], F32))
        PS2 = E(nc.psum_tensor("PS2", [128, 512], F32))
        PS3 = E(nc.psum_tensor("PS3", [128, 512], F32))
        PS1 = E(nc.psum_tensor("PS1", [128, 512], F32))
        PS1B = E(nc.psum_tensor("PS1B", [128, 512], BF16))
        sem = E(nc.semaphore("chain"))
        block = E(nc.Block())

        b = B(nc)
        F_, K1 = RSC[:, 0:32], RSC[:, 32:64]
        COS, SIN = RSC[:, 64:96], RSC[:, 96:128]
        T1, T2 = RSC[:, 128:160], RSC[:, 160:192]
        T3, T4, TMPA = RSC[:, 192:224], RSC[:, 224:256], RSC[:, 0:32]

        emit_identity(b, ID)
        emit_identity(b, IDF)
        for kc in range(16):
            b.dma(WQA[:, kc * Q_LORA:(kc + 1) * Q_LORA],
                  wqaT_in[kc * 128:(kc + 1) * 128, :], eng="gpsimd")
            b.dma(WKVA[:, kc * 576:(kc + 1) * 576],
                  wkvaT_in[kc * 128:(kc + 1) * 128, :], eng="gpsimd")
        for kc in range(12):
            b.dma(WQB[:, kc * NH * QH:(kc + 1) * NH * QH],
                  wqbT_in[kc * 128:(kc + 1) * 128, :], eng="gpsimd")
        b.dma(TMP[0:1, 0:Q_LORA], qlnw_in[0:1, :], eng="gpsimd")
        b.bcast_rows(QLNW, TMP[0:1, 0:Q_LORA])
        b.dma(TMP[0:1, 0:KV_LORA], kvlnw_in[0:1, :], eng="gpsimd")
        b.bcast_rows(KVLNW, TMP[0:1, 0:KV_LORA])
        b.dma(TMP[0:1, 0:32], invf_in[0:1, :], eng="gpsimd")
        b.bcast_rows(IVB, TMP[0:1, 0:32])

        for i in range(NT):
            rows = slice(i * 128, (i + 1) * 128)
            b.dma(XT[:, :], x_in[rows, :])
            b.dma(POSC[:, :], pos_in[rows, :])
            emit_rope_tables(b, POSC, IVB, F_, K1, COS, SIN, TMPA)
            emit_ln_tm(b, XT[:, :], LN1B, COLS, D, 1e-5, sub_mean=True,
                       sq_scratch=TMP)
            for kc in range(16):
                b.tp(PS1B[:, 0:128], LN1B[:, kc * 128:(kc + 1) * 128],
                     LN1F[:, kc * 128:(kc + 1) * 128], ID[:])
            for nb in range(3):
                b.mm_acc(PS0[:, 0:512],
                         [(LN1F[:, kc * 128:(kc + 1) * 128],
                           WQA[:, kc * Q_LORA + nb * 512:
                               kc * Q_LORA + (nb + 1) * 512])
                          for kc in range(16)])
                b.cp(QL[:, nb * 512:(nb + 1) * 512], PS0[:, 0:512])
            b.act(TMP[:, 0:Q_LORA], QL[:, :], AF.Square, accum_out=COLS[:, 1:2])
            b.ts(COLS[:, 1:2], COLS[:, 1:2], 1.0 / Q_LORA)
            b.ts(COLS[:, 1:2], COLS[:, 1:2], 1e-6, op0=ALU.add)
            b.act(COLS[:, 0:1], COLS[:, 1:2], AF.Sqrt)
            b.op("vector", "reciprocal", COLS[:, 3:4], COLS[:, 0:1])
            b.ts(QL[:, :], QL[:, :], COLS[:, 3:4], op0=ALU.mult)
            b.tt(QLB[:, :], QL[:, :], QLNW[:, :], ALU.mult)
            for kc in range(12):
                b.tp(PS1B[:, 0:128], QLB[:, kc * 128:(kc + 1) * 128],
                     QLF[:, kc * 128:(kc + 1) * 128], ID[:])
            for nb in range(6):
                b.mm_acc(PS0[:, 0:512],
                         [(QLF[:, kc * 128:(kc + 1) * 128],
                           WQB[:, kc * NH * QH + nb * 512:
                               kc * NH * QH + (nb + 1) * 512])
                          for kc in range(12)])
                b.ts(QA[:, nb * 512:(nb + 1) * 512], PS0[:, 0:512], SCALE,
                     op0=ALU.mult)
            for h in range(NH):
                x1 = QA[:, h * QH + NOPE:h * QH + NOPE + 32]
                x2 = QA[:, h * QH + NOPE + 32:h * QH + QH]
                emit_rope_apply(b, x1, x2, COS, SIN, T1, T2, T3, T4)
            b.cp(QAB[:, :], QA[:, :])
            b.dma(q_out[rows, :], QAB[:, :])
            b.mm_acc(PS0[:, 0:512],
                     [(LN1F[:, kc * 128:(kc + 1) * 128],
                       WKVA[:, kc * 576: kc * 576 + 512])
                      for kc in range(16)])
            b.cp(KV[:, 0:512], PS0[:, 0:512])
            b.mm_acc(PS2[:, 0:64],
                     [(LN1F[:, kc * 128:(kc + 1) * 128],
                       WKVA[:, kc * 576 + 512: kc * 576 + 576])
                      for kc in range(16)])
            b.cp(KV[:, 512:576], PS2[:, 0:64])
            b.act(TMP[:, 0:512], KV[:, 0:512], AF.Square,
                  accum_out=COLS[:, 1:2])
            b.ts(COLS[:, 1:2], COLS[:, 1:2], 1.0 / KV_LORA)
            b.ts(COLS[:, 1:2], COLS[:, 1:2], 1e-6, op0=ALU.add)
            b.act(COLS[:, 0:1], COLS[:, 1:2], AF.Sqrt)
            b.op("vector", "reciprocal", COLS[:, 3:4], COLS[:, 0:1])
            b.ts(KV[:, 0:512], KV[:, 0:512], COLS[:, 3:4], op0=ALU.mult)
            b.tt(KVB[:, 0:512], KV[:, 0:512], KVLNW[:, :], ALU.mult)
            emit_rope_apply(b, KV[:, 512:544], KV[:, 544:576], COS, SIN,
                            T1, T2, T3, T4)
            b.cp(KVB[:, 512:576], KV[:, 512:576])
            b.dma(kvn_out[rows, :], KVB[:, :])

        b.emit(block, sem)
    return nc


def build_L1b():
    nc = bass.Bass("TRN2", target_bir_lowering=False, debug=False)
    qT_in = _io(nc, "qT", [HPC * QH, S], BF16, "ExternalInput")
    kvT_in = _io(nc, "kvT", [576, S], BF16, "ExternalInput")
    qabsT_in = _io(nc, "qabsT", [HPC * 128, 4 * NOPE], F32, "ExternalInput")
    oaT_in = _io(nc, "oaT", [HPC * 128, 4 * VH], F32, "ExternalInput")
    xh_out = _io(nc, "xh_out", [S, HPC * VH], F32, "ExternalOutput")

    NKB = S // 512

    with contextlib.ExitStack() as ctx:
        E = ctx.enter_context
        QNF = E(nc.sbuf_tensor("QNF", [128, S], BF16))
        QPF = E(nc.sbuf_tensor("QPF", [64, S], BF16))
        KH = E(nc.sbuf_tensor("KH", [128, S], BF16))
        VHT = E(nc.sbuf_tensor("VHT", [128, S], BF16))
        KPF = E(nc.sbuf_tensor("KPF", [64, S], BF16))
        KVF = E(nc.sbuf_tensor("KVF", [128, 512], BF16))
        QT = E(nc.sbuf_tensor("QT", [128, 4 * 128], BF16))
        OT = E(nc.sbuf_tensor("OT", [128, 4 * 128], BF16))
        SROW = E(nc.sbuf_tensor("SROW", [128, S], F32))
        PB = E(nc.sbuf_tensor("PB", [128, S], BF16))
        PTB = E(nc.sbuf_tensor("PTB", [128, 128], BF16))
        MX = E(nc.sbuf_tensor("MX", [128, NKB], F32))
        LC = E(nc.sbuf_tensor("LC", [128, NKB], F32))
        COLS = E(nc.sbuf_tensor("COLS", [128, 8], F32))
        MASK = E(nc.sbuf_tensor("MASK", [128, 4 * 512], F32))
        XH = E(nc.sbuf_tensor("XH", [128, VH], F32))
        ID = E(nc.sbuf_tensor("ID", [128, 128], BF16))
        PSS = E(nc.psum_tensor("PSS", [128, 512], F32))
        PSK = E(nc.psum_tensor("PSK", [128, 512], F32))
        PSV = E(nc.psum_tensor("PSV", [128, 512], F32))
        PSO = E(nc.psum_tensor("PSO", [128, 512], F32))
        PS1B = E(nc.psum_tensor("PS1B", [128, 512], BF16))
        sem = E(nc.semaphore("chain"))
        block = E(nc.Block())

        b = B(nc)
        emit_identity(b, ID)
        for qq in range(4):
            m = MASK[:, qq * 512:(qq + 1) * 512]
            b.op("vector", "memset", m, 0.0)
            b.op("gpsimd", "affine_select", m, m, pattern=[[-1, 512]],
                 compare_op=ALU.is_ge, fill=-30000.0, base=qq * 128,
                 channel_multiplier=1)
        b.dma(KPF[0:64, :], kvT_in[512:576, :])

        for hh in range(HPC):
            b.dma(QNF[:, :], qT_in[hh * QH:hh * QH + NOPE, :])
            b.dma(QPF[0:64, :], qT_in[hh * QH + NOPE:hh * QH + QH, :])
            b.dma(QT[:, :], qabsT_in[hh * 128:(hh + 1) * 128, :], eng="gpsimd")
            b.dma(OT[:, :], oaT_in[hh * 128:(hh + 1) * 128, :], eng="gpsimd")
            for tb in range(NKB):
                cols = slice(tb * 512, (tb + 1) * 512)
                for cc in range(4):
                    b.dma(KVF[:, 0:512], kvT_in[cc * 128:(cc + 1) * 128, cols])
                    b.mm(PSK[:, 0:512], QT[:, cc * 128:(cc + 1) * 128],
                         KVF[:, 0:512], start=(cc == 0), stop=(cc == 3))
                    b.mm(PSV[:, 0:512], OT[:, cc * 128:(cc + 1) * 128],
                         KVF[:, 0:512], start=(cc == 0), stop=(cc == 3))
                b.cp(KH[:, cols], PSK[:, 0:512])
                b.cp(PB[:, 0:512], PSV[:, 0:512])
                for j in range(4):
                    ct = tb * 4 + j
                    b.tp(PS1B[:, 0:128], PB[:, j * 128:(j + 1) * 128],
                         VHT[:, ct * 128:(ct + 1) * 128], ID[:])

            for qi in range(NTF):
                nk = qi // 4 + 1
                qcols = slice(qi * 128, (qi + 1) * 128)
                for kb in range(nk):
                    kcols = slice(kb * 512, (kb + 1) * 512)
                    b.mm(PSS[:, 0:512], QNF[:, qcols], KH[:, kcols],
                         start=True, stop=False)
                    b.mm(PSS[:, 0:512], QPF[0:64, qcols], KPF[0:64, kcols],
                         start=False, stop=True)
                    if kb == nk - 1:
                        b.tt(PSS[:, 0:512], PSS[:, 0:512],
                             MASK[:, (qi % 4) * 512:(qi % 4 + 1) * 512],
                             ALU.add)
                    b.op("vector", "reduce_max", MX[:, kb:kb + 1],
                         PSS[:, 0:512], axis=AX.X)
                    b.cp(SROW[:, kcols], PSS[:, 0:512])
                b.op("vector", "reduce_max", COLS[:, 0:1], MX[:, 0:nk],
                     axis=AX.X, negate=True)
                for kb in range(nk):
                    kcols = slice(kb * 512, (kb + 1) * 512)
                    b.act(PB[:, kcols], SROW[:, kcols], AF.Exp,
                          bias=COLS[:, 0:1], accum_out=LC[:, kb:kb + 1])
                b.op("vector", "reduce_sum", COLS[:, 1:2], LC[:, 0:nk],
                     axis=AX.X)
                b.op("vector", "reciprocal", COLS[:, 2:3], COLS[:, 1:2])
                nch = nk * 4
                for ct in range(nch):
                    b.tp(PS1B[:, 0:128], PB[:, ct * 128:(ct + 1) * 128],
                         PTB[:, :], ID[:])
                    b.mm(PSO[:, 0:VH], PTB[:, :],
                         VHT[:, ct * 128:(ct + 1) * 128],
                         start=(ct == 0), stop=(ct == nch - 1))
                b.ts(XH[:, :], PSO[:, 0:VH], COLS[:, 2:3], op0=ALU.mult)
                b.dma(xh_out[qcols, hh * VH:(hh + 1) * VH], XH[:, :])

        b.emit(block, sem)
    return nc


def build_L3():
    nc = bass.Bass("TRN2", target_bir_lowering=False, debug=False)
    lnxT_in = _io(nc, "lnxT", [D, S], BF16, "ExternalInput")
    wq_in = _io(nc, "wq", [D, 1544], F32, "ExternalInput")
    convw_in = _io(nc, "convw", [128, 32], F32, "ExternalInput")
    alog_in = _io(nc, "alog", [1, 4], F32, "ExternalInput")
    dtb_in = _io(nc, "dtb", [1, 4], F32, "ExternalInput")
    normw_in = _io(nc, "normw", [1, 128], F32, "ExternalInput")
    outb = _io(nc, "outb", [S, VPC * DV], BF16, "ExternalOutput")

    SC = 1024
    NSC = S // SC

    with contextlib.ExitStack() as ctx:
        E = ctx.enter_context
        WQ = E(nc.sbuf_tensor("WQ", [128, 16 * 1544], BF16))
        LNXF = E(nc.sbuf_tensor("LNXF", [128, 16 * 512], BF16))
        CONVF = E(nc.sbuf_tensor("CONVF", [128, 8 * SC], F32))
        ZF = E(nc.sbuf_tensor("ZF", [128, 4 * SC], BF16))
        BAF = E(nc.sbuf_tensor("BAF", [8, SC], F32))
        XEXT = E(nc.sbuf_tensor("XEXT", [128, SC + 3], F32))
        CARRY = E(nc.sbuf_tensor("CARRY", [128, 8 * 3], F32))
        CONVW = E(nc.sbuf_tensor("CONVW", [128, 32], F32))
        QKVTM = E(nc.sbuf_tensor("QKVTM", [128, 8 * 128], F32))
        ZTM = E(nc.sbuf_tensor("ZTM", [128, 4 * 128], BF16))
        BATM = E(nc.sbuf_tensor("BATM", [128, 8], F32))
        QNB = E(nc.sbuf_tensor("QNB", [128, 2 * 128], BF16))
        KNB = E(nc.sbuf_tensor("KNB", [128, 2 * 128], BF16))
        VB = E(nc.sbuf_tensor("VB", [128, 4 * 128], BF16))
        G4 = E(nc.sbuf_tensor("G4", [128, 4], F32))
        BETA4 = E(nc.sbuf_tensor("BETA4", [128, 4], F32))
        NEGB4 = E(nc.sbuf_tensor("NEGB4", [128, 4], F32))
        SF = E(nc.sbuf_tensor("SF", [128, 4 * 128], F32))
        SB = E(nc.sbuf_tensor("SB", [128, 4 * 128], BF16))
        MKK = E(nc.sbuf_tensor("MKK", [128, 128], F32))
        GRAW = E(nc.sbuf_tensor("GRAW", [128, 128], F32))
        ES = E(nc.sbuf_tensor("ES", [128, 128], F32))
        ETI = E(nc.sbuf_tensor("ETI", [128, 128], F32))
        NMAT = E(nc.sbuf_tensor("NMAT", [128, 128], BF16))
        NTMAT = E(nc.sbuf_tensor("NTMAT", [128, 128], BF16))
        NP2 = E(nc.sbuf_tensor("NP2", [128, 128], BF16))
        NTP2 = E(nc.sbuf_tensor("NTP2", [128, 128], BF16))
        UB = E(nc.sbuf_tensor("UB", [128, 128], BF16))
        VBETA = E(nc.sbuf_tensor("VBETA", [128, 128], F32))
        KF = E(nc.sbuf_tensor("KF", [128, 128], BF16))
        QF = E(nc.sbuf_tensor("QF", [128, 128], BF16))
        KSC_ = E(nc.sbuf_tensor("KSC", [128, 128], BF16))
        KSF = E(nc.sbuf_tensor("KSF", [128, 128], BF16))
        KRB = E(nc.sbuf_tensor("KRB", [128, 128], BF16))
        GTB = E(nc.sbuf_tensor("GTB", [128, 128], BF16))
        OFF = E(nc.sbuf_tensor("OFF", [128, 128], F32))
        OG = E(nc.sbuf_tensor("OG", [128, 128], BF16))
        GROW = E(nc.sbuf_tensor("GROW", [1, 128], F32))
        SROW1 = E(nc.sbuf_tensor("SROW1", [1, 128], F32))
        SCOL = E(nc.sbuf_tensor("SCOL", [128, 1], F32))
        NSCOL = E(nc.sbuf_tensor("NSCOL", [128, 1], F32))
        BSC = E(nc.sbuf_tensor("BSC", [128, 1], F32))
        BCOL = E(nc.sbuf_tensor("BCOL", [128, 1], F32))
        RCOL = E(nc.sbuf_tensor("RCOL", [128, 1], F32))
        COLS = E(nc.sbuf_tensor("COLS", [128, 8], F32))
        C1 = E(nc.sbuf_tensor("C1", [128, 1], F32))
        ONES1 = E(nc.sbuf_tensor("ONES1", [1, 128], F32))
        C1R = E(nc.sbuf_tensor("C1R", [1, 1], F32))
        E127 = E(nc.sbuf_tensor("E127", [128, 128], F32))
        MADDA = E(nc.sbuf_tensor("MADDA", [128, 128], F32))
        MADDB = E(nc.sbuf_tensor("MADDB", [128, 128], F32))
        NEGA = E(nc.sbuf_tensor("NEGA", [128, 4], F32))
        DTB = E(nc.sbuf_tensor("DTB", [128, 4], F32))
        NWB = E(nc.sbuf_tensor("NWB", [128, 128], F32))
        ID = E(nc.sbuf_tensor("ID", [128, 128], BF16))
        IDF = E(nc.sbuf_tensor("IDF", [128, 128], F32))
        PSQ = E(nc.psum_tensor("PSQ", [128, 512], F32))
        PSOU = E(nc.psum_tensor("PSOU", [128, 512], F32))
        PSA = E(nc.psum_tensor("PSA", [128, 512], F32))
        PSM = E(nc.psum_tensor("PSM", [128, 512], F32))
        PSM2 = E(nc.psum_tensor("PSM2", [128, 512], F32))
        PS1 = E(nc.psum_tensor("PS1", [128, 512], F32))
        PS1B = E(nc.psum_tensor("PS1B", [128, 512], BF16))
        sem = E(nc.semaphore("chain"))
        block = E(nc.Block())

        b = B(nc)
        emit_identity(b, ID)
        emit_identity(b, IDF)
        b.op("vector", "memset", C1[:], 1.0)
        nc.const_aps.register(F32, 1.0, C1[:, 0:1].opt())
        b.op("vector", "memset", ONES1[0:1, :], 1.0)
        b.op("vector", "memset", C1R[0:1, :], 1.0)
        b.op("vector", "memset", E127[:], 1.0)
        b.op("gpsimd", "affine_select", E127[:], E127[:], pattern=[[0, 128]],
             compare_op=ALU.is_ge, fill=0.0, base=-127, channel_multiplier=1)
        b.op("vector", "memset", MADDA[:], 0.0)
        b.op("gpsimd", "affine_select", MADDA[:], MADDA[:],
             pattern=[[-1, 128]], compare_op=ALU.is_ge, fill=1e4, base=-1,
             channel_multiplier=1)
        b.op("vector", "memset", MADDB[:], 0.0)
        b.op("gpsimd", "affine_select", MADDB[:], MADDB[:],
             pattern=[[1, 128]], compare_op=ALU.is_ge, fill=-1e4, base=0,
             channel_multiplier=-1)
        b.op("vector", "memset", CARRY[:], 0.0)
        b.op("vector", "memset", SF[:], 0.0)
        b.op("vector", "memset", SB[:], 0.0)
        for kc in range(16):
            b.dma(WQ[:, kc * 1544:(kc + 1) * 1544],
                  wq_in[kc * 128:(kc + 1) * 128, :], eng="gpsimd")
        b.dma(CONVW[:, :], convw_in[:, :])
        b.dma(NEGA[0:1, :], alog_in[0:1, :])
        b.act(NEGA[0:1, :], NEGA[0:1, :], AF.Exp)
        b.ts(NEGA[0:1, :], NEGA[0:1, :], -1.0)
        b.bcast_rows(NEGA, NEGA[0:1, :])
        b.dma(DTB[0:1, :], dtb_in[0:1, :])
        b.bcast_rows(DTB, DTB[0:1, :])
        b.dma(NWB[0:1, :], normw_in[0:1, :])
        b.bcast_rows(NWB, NWB[0:1, :])

        for sc in range(NSC):
            for t2 in range(2):
                tcols = slice(sc * SC + t2 * 512, sc * SC + (t2 + 1) * 512)
                for kc in range(16):
                    b.dma(LNXF[:, kc * 512:(kc + 1) * 512],
                          lnxT_in[kc * 128:(kc + 1) * 128, tcols])
                for cc in range(12):
                    b.mm_acc(PSQ[:, 0:512],
                             [(WQ[:, kc * 1544 + cc * 128:
                                  kc * 1544 + (cc + 1) * 128],
                               LNXF[:, kc * 512:(kc + 1) * 512])
                              for kc in range(16)])
                    if cc < 8:
                        b.cp(CONVF[:, cc * SC + t2 * 512:
                                   cc * SC + (t2 + 1) * 512], PSQ[:, 0:512])
                    else:
                        zc = cc - 8
                        b.act(ZF[:, zc * SC + t2 * 512:
                                 zc * SC + (t2 + 1) * 512], PSQ[:, 0:512],
                              AF.Silu)
                b.mm_acc(PSQ[0:8, 0:512],
                         [(WQ[:, kc * 1544 + 1536: kc * 1544 + 1544],
                           LNXF[:, kc * 512:(kc + 1) * 512])
                          for kc in range(16)])
                b.cp(BAF[0:8, t2 * 512:(t2 + 1) * 512], PSQ[0:8, 0:512])
            for chm in range(8):
                b.cp(XEXT[:, 0:3], CARRY[:, chm * 3:(chm + 1) * 3])
                b.cp(XEXT[:, 3:SC + 3], CONVF[:, chm * SC:(chm + 1) * SC])
                b.cp(CARRY[:, chm * 3:(chm + 1) * 3], XEXT[:, SC:SC + 3])
                dst = CONVF[:, chm * SC:(chm + 1) * SC]
                b.ts(dst, XEXT[:, 0:SC], CONVW[:, chm * 4:chm * 4 + 1],
                     op0=ALU.mult)
                for j in range(1, 4):
                    b.stt(dst, XEXT[:, j:j + SC],
                          CONVW[:, chm * 4 + j:chm * 4 + j + 1], dst,
                          op0=ALU.mult, op1=ALU.add)
                b.act(dst, dst, AF.Silu)

            for ti in range(8):
                ck = sc * 8 + ti
                rows = slice(ck * 128, (ck + 1) * 128)
                for chm in range(8):
                    b.tp(PS1[:, 0:128],
                         CONVF[:, chm * SC + ti * 128: chm * SC + (ti + 1) * 128],
                         QKVTM[:, chm * 128:(chm + 1) * 128], IDF[:])
                for vh in range(4):
                    b.tp(PS1B[:, 0:128],
                         ZF[:, vh * SC + ti * 128: vh * SC + (ti + 1) * 128],
                         ZTM[:, vh * 128:(vh + 1) * 128], ID[:])
                b.op("tensor", "transpose", PS1[:, 0:8],
                     BAF[0:8, ti * 128:(ti + 1) * 128], IDF[0:8, 0:8])
                b.cp(BATM[:, 0:8], PS1[:, 0:8])
                for j in range(4):
                    sl = QKVTM[:, j * 128:(j + 1) * 128]
                    b.act(OFF[:, :], sl, AF.Square, accum_out=COLS[:, 0:1])
                    b.ts(COLS[:, 0:1], COLS[:, 0:1], 1e-6, op0=ALU.add)
                    b.act(COLS[:, 1:2], COLS[:, 0:1], AF.Sqrt)
                    b.op("vector", "reciprocal", COLS[:, 2:3], COLS[:, 1:2])
                    if j < 2:
                        b.ts(QNB[:, j * 128:(j + 1) * 128], sl, COLS[:, 2:3],
                             float(DK ** -0.5), op0=ALU.mult, op1=ALU.mult)
                    else:
                        b.ts(KNB[:, (j - 2) * 128:(j - 1) * 128], sl,
                             COLS[:, 2:3], op0=ALU.mult)
                for vh in range(4):
                    b.cp(VB[:, vh * 128:(vh + 1) * 128],
                         QKVTM[:, (4 + vh) * 128:(5 + vh) * 128])
                for vh in range(4):
                    kh = vh // 2
                    bcol = BATM[:, kh * 4 + vh % 2: kh * 4 + vh % 2 + 1]
                    acol = BATM[:, kh * 4 + 2 + vh % 2: kh * 4 + 3 + vh % 2]
                    t_ = COLS[:, 3:4]
                    b.tt(t_, acol, DTB[:, vh:vh + 1], ALU.add)
                    b.act(COLS[:, 4:5], t_, AF.Abs)
                    b.act(COLS[:, 4:5], COLS[:, 4:5], AF.Exp, scale=-1.0)
                    b.act(COLS[:, 4:5], COLS[:, 4:5], AF.Ln, bias=1.0)
                    b.ts(COLS[:, 5:6], t_, 0.0, op0=ALU.max)
                    b.tt(COLS[:, 4:5], COLS[:, 4:5], COLS[:, 5:6], ALU.add)
                    b.tt(G4[:, vh:vh + 1], COLS[:, 4:5], NEGA[:, vh:vh + 1],
                         ALU.mult)
                    b.act(BETA4[:, vh:vh + 1], bcol, AF.Sigmoid)
                b.ts(NEGB4[:], BETA4[:], -1.0)

                for vh in range(4):
                    kh = vh // 2
                    ktm = KNB[:, kh * 128:(kh + 1) * 128]
                    qtm = QNB[:, kh * 128:(kh + 1) * 128]
                    if vh % 2 == 0:
                        b.tp(PS1B[:, 0:128], ktm, KF[:, :], ID[:])
                        b.tp(PS1B[:, 0:128], qtm, QF[:, :], ID[:])
                        b.mm(PSM[:, 0:128], KF[:, :], KF[:, :], True, True)
                        b.cp(MKK[:, :], PSM[:, 0:128])
                        b.mm(PSM[:, 0:128], KF[:, :], QF[:, :], True, True)
                        b.cp(GRAW[:, :], PSM[:, 0:128])
                    b.op("tensor", "transpose", PS1[0:1, 0:128],
                         G4[:, vh:vh + 1], IDF[:])
                    b.cp(GROW[0:1, :], PS1[0:1, 0:128])
                    b.op("vector", "tensor_tensor_scan", SROW1[0:1, :],
                         GROW[0:1, :], GROW[0:1, :], 0.0, op0=ALU.add,
                         op1=ALU.bypass)
                    b.mm(PSOU[:, 0:1], SROW1[0:1, :], C1R[0:1, 0:1], True, True)
                    b.cp(SCOL[:, :], PSOU[:, 0:1])
                    b.ts(NSCOL[:], SCOL[:], -1.0)
                    b.mm(PSOU[:, 0:1], E127[:, :], SCOL[:, :], True, True)
                    b.cp(BSC[:, :], PSOU[:, 0:1])
                    b.act(BCOL[:, :], BSC[:, :], AF.Exp)
                    b.act(RCOL[:, :], SCOL[:, :], AF.Exp, scale=-1.0,
                          bias=BSC[:, 0:1])
                    b.mm(PSOU[:, 0:128], ONES1[0:1, :], SROW1[0:1, :],
                         True, False)
                    b.mm(PSOU[:, 0:128], IDF[:, :], MADDA[:, :], False, True)
                    b.act(ES[:, :], PSOU[:, 0:128], AF.Exp, scale=-1.0,
                          bias=SCOL[:, 0:1])
                    b.mm(PSOU[:, 0:128], ONES1[0:1, :], SROW1[0:1, :],
                         True, False)
                    b.mm(PSOU[:, 0:128], IDF[:, :], MADDB[:, :], False, True)
                    b.act(ETI[:, :], PSOU[:, 0:128], AF.Exp, scale=1.0,
                          bias=NSCOL[:, 0:1])
                    b.stt(NMAT[:, :], MKK[:, :], NEGB4[:, vh:vh + 1], ES[:, :],
                          op0=ALU.mult, op1=ALU.mult)
                    b.tp(PS1B[:, 0:128], NMAT[:, :], NTMAT[:, :], ID[:])
                    b.ts(KSC_[:, :], KNB[:, (vh // 2) * 128:(vh // 2 + 1) * 128],
                         BCOL[:, 0:1], op0=ALU.mult)
                    b.tp(PS1B[:, 0:128], KSC_[:, :], KSF[:, :], ID[:])
                    b.mm(PSM[:, 0:128], KSF[:, :],
                         SB[:, vh * 128:(vh + 1) * 128], True, True)
                    b.ts(VBETA[:, :], VB[:, vh * 128:(vh + 1) * 128],
                         BETA4[:, vh:vh + 1], op0=ALU.mult)
                    b.stt(UB[:, :], PSM[:, 0:128], NEGB4[:, vh:vh + 1],
                          VBETA[:, :], op0=ALU.mult, op1=ALU.add)
                    npow, ntpow = NMAT, NTMAT
                    np2, ntp2 = NP2, NTP2
                    for lvl in range(7):
                        b.mm(PSA[:, 0:128], ntpow[:, :], UB[:, :], True, False)
                        b.mm(PSA[:, 0:128], ID[:, :], UB[:, :], False, True)
                        b.cp(UB[:, :], PSA[:, 0:128])
                        if lvl < 6:
                            b.mm(PSA[:, 0:128], npow[:, :], ntpow[:, :],
                                 True, True)
                            b.cp(ntp2[:, :], PSA[:, 0:128])
                            if lvl < 5:
                                b.mm(PSA[:, 0:128], ntpow[:, :], npow[:, :],
                                     True, True)
                                b.cp(np2[:, :], PSA[:, 0:128])
                            npow, np2 = np2, npow
                            ntpow, ntp2 = ntp2, ntpow
                    b.tt(GTB[:, :], GRAW[:, :], ETI[:, :], ALU.mult)
                    b.mm(PSM[:, 0:128], QF[:, :],
                         SB[:, vh * 128:(vh + 1) * 128], True, True)
                    b.ts(OFF[:, :], PSM[:, 0:128], BCOL[:, 0:1], op0=ALU.mult)
                    b.mm(PSM2[:, 0:128], GTB[:, :], UB[:, :], True, True)
                    b.tt(OFF[:, :], OFF[:, :], PSM2[:, 0:128], ALU.add)
                    b.ts(KRB[:, :], KNB[:, (vh // 2) * 128:(vh // 2 + 1) * 128],
                         RCOL[:, 0:1], op0=ALU.mult)
                    b.mm(PSM[:, 0:128], KRB[:, :], UB[:, :], True, True)
                    b.stt(SF[:, vh * 128:(vh + 1) * 128],
                          SF[:, vh * 128:(vh + 1) * 128], BCOL[:, 0:1],
                          PSM[:, 0:128], op0=ALU.mult, op1=ALU.add)
                    b.cp(SB[:, vh * 128:(vh + 1) * 128],
                         SF[:, vh * 128:(vh + 1) * 128])
                    b.act(VBETA[:, :], OFF[:, :], AF.Square,
                          accum_out=COLS[:, 0:1])
                    b.ts(COLS[:, 0:1], COLS[:, 0:1], 1.0 / DV)
                    b.ts(COLS[:, 0:1], COLS[:, 0:1], 1e-6, op0=ALU.add)
                    b.act(COLS[:, 1:2], COLS[:, 0:1], AF.Sqrt)
                    b.op("vector", "reciprocal", COLS[:, 2:3], COLS[:, 1:2])
                    b.ts(OFF[:, :], OFF[:, :], COLS[:, 2:3], op0=ALU.mult)
                    b.tt(OFF[:, :], OFF[:, :], NWB[:, :], ALU.mult)
                    b.tt(OG[:, :], OFF[:, :], ZTM[:, vh * 128:(vh + 1) * 128],
                         ALU.mult)
                    b.dma(outb[rows, vh * 128:(vh + 1) * 128], OG[:, :])

        b.emit(block, sem)
    return nc


_ROPE_PERM = np.concatenate([np.arange(0, 64, 2), np.arange(1, 64, 2)])


def _stage_lhsT(A):
    return np.ascontiguousarray(
        A.T.reshape(4, 128, 128).transpose(1, 0, 2).reshape(128, 512))


_BUILD_CACHE = {}


def _get(name, fn):
    if name not in _BUILD_CACHE:
        _BUILD_CACHE[name] = fn()
    return _BUILD_CACHE[name]


def kernel(L0_attn_a_hidden_states, L0_attn_a_wq_a, L0_attn_a_q_a_ln_w,
           L0_attn_a_wq_b, L0_attn_a_wkv_a, L0_attn_a_kv_a_ln_w,
           L0_attn_a_wkv_b, L0_attn_a_wo, L0_attn_b_W_qkvz, L0_attn_b_W_ba,
           L0_attn_b_conv_weight, L0_attn_b_A_log, L0_attn_b_dt_bias,
           L0_attn_b_norm_weight, L0_attn_b_W_out, L0_ffn_weight,
           L0_ffn_bias):
    cores = list(range(NCORES))
    asc = np.ascontiguousarray
    x = asc(np.asarray(L0_attn_a_hidden_states)[0])

    perm = _ROPE_PERM
    wqb = np.asarray(L0_attn_a_wq_b)
    cols = []
    for h in range(NH):
        cols.extend(range(h * QH, h * QH + NOPE))
        cols.extend([h * QH + NOPE + p for p in perm])
    wqbT = asc(wqb[cols, :].T)
    wkva = np.asarray(L0_attn_a_wkv_a)
    rows = list(range(KV_LORA)) + [KV_LORA + p for p in perm]
    wkvaT = asc(wkva[rows, :].T)
    wqaT = asc(np.asarray(L0_attn_a_wq_a).T)
    invf = (1.0 / 10000.0 ** (np.arange(0, ROPE_D, 2, dtype=np.float32)
                              / ROPE_D)).reshape(1, 32).astype(np.float32)
    qlnw = asc(np.asarray(L0_attn_a_q_a_ln_w).reshape(1, -1))
    kvlnw = asc(np.asarray(L0_attn_a_kv_a_ln_w).reshape(1, -1))
    nc1a = _get("L1a", build_L1a)
    maps = []
    for c in cores:
        sl = slice(c * TS, (c + 1) * TS)
        maps.append({
            "x_in": asc(x[sl]),
            "pos": np.arange(c * TS, (c + 1) * TS,
                             dtype=np.float32).reshape(-1, 1),
            "invf": invf, "wqaT": wqaT, "qlnw": qlnw, "wqbT": wqbT,
            "wkvaT": wkvaT, "kvlnw": kvlnw})
    res = run_bass_kernel_spmd(nc1a, maps, core_ids=cores)
    q_all = np.concatenate([r["q_out"] for r in res.results], 0)
    kvn = np.concatenate([r["kvn_out"] for r in res.results], 0)

    w_kvb = np.asarray(L0_attn_a_wkv_b).reshape(NH, NOPE + VH, KV_LORA)
    kvT = asc(kvn.T)
    nc1b = _get("L1b", build_L1b)
    maps = []
    for c in cores:
        hcols = slice(HPC * c * QH, (HPC * c + HPC) * QH)
        qabsT = np.concatenate(
            [_stage_lhsT(w_kvb[HPC * c + hh, :NOPE, :]) for hh in range(HPC)],
            0).astype(np.float32)
        oaT = np.concatenate(
            [_stage_lhsT(w_kvb[HPC * c + hh, VH:, :]) for hh in range(HPC)],
            0).astype(np.float32)
        maps.append({"qT": asc(q_all[:, hcols].T), "kvT": kvT,
                     "qabsT": qabsT, "oaT": oaT})
    res = run_bass_kernel_spmd(nc1b, maps, core_ids=cores)
    xh = np.concatenate([r["xh_out"] for r in res.results], 1)

    woT = asc(np.asarray(L0_attn_a_wo).T)
    fw = asc(np.asarray(L0_ffn_weight))
    fwT = asc(fw.T)
    fb = asc(np.asarray(L0_ffn_bias).reshape(1, -1))
    nc2 = _get("L2", lambda: build_L2(first=True))
    maps = []
    for c in cores:
        sl = slice(c * TS, (c + 1) * TS)
        maps.append({"x_in": asc(x[sl]), "xa": asc(xh[sl]), "wT": woT,
                     "fw": fw, "fwT": fwT, "fb": fb})
    res = run_bass_kernel_spmd(nc2, maps, core_ids=cores)
    x2 = np.concatenate([r["x_out"] for r in res.results], 0)
    ln3 = np.concatenate([r["ln_out"] for r in res.results], 0)

    Wq = np.asarray(L0_attn_b_W_qkvz)
    Wba = np.asarray(L0_attn_b_W_ba)
    cw = np.asarray(L0_attn_b_conv_weight)
    alog = np.asarray(L0_attn_b_A_log)
    dtb = np.asarray(L0_attn_b_dt_bias)
    normw = asc(np.asarray(L0_attn_b_norm_weight).reshape(1, DV))
    lnxT = asc(ln3.T)
    nc3 = _get("L3", build_L3)
    maps = []
    for c in cores:
        khs = [KPC * c, KPC * c + 1]
        qc = np.concatenate([np.arange(kh * 768, kh * 768 + 128)
                             for kh in khs])
        kc = np.concatenate([np.arange(kh * 768 + 128, kh * 768 + 256)
                             for kh in khs])
        vc = np.concatenate([np.arange(kh * 768 + 256, kh * 768 + 512)
                             for kh in khs])
        zc = np.concatenate([np.arange(kh * 768 + 512, kh * 768 + 768)
                             for kh in khs])
        bac = np.concatenate([np.arange(kh * 4, kh * 4 + 4) for kh in khs])
        wq_stage = asc(np.concatenate(
            [Wq[:, qc], Wq[:, kc], Wq[:, vc], Wq[:, zc], Wba[:, bac]], 1))
        conv_rows = np.concatenate([
            np.arange(KPC * c * 128, (KPC * c + 2) * 128),
            2048 + np.arange(KPC * c * 128, (KPC * c + 2) * 128),
            4096 + np.arange(VPC * c * 128, (VPC * c + 4) * 128)])
        convw = asc(cw[conv_rows].reshape(8, 128, 4)
                    .transpose(1, 0, 2).reshape(128, 32))
        maps.append({
            "lnxT": lnxT, "wq": wq_stage, "convw": convw,
            "alog": asc(alog[VPC * c:VPC * c + 4].reshape(1, 4)),
            "dtb": asc(dtb[VPC * c:VPC * c + 4].reshape(1, 4)),
            "normw": normw})
    res = run_bass_kernel_spmd(nc3, maps, core_ids=cores)
    outb = np.concatenate([r["outb"] for r in res.results], 1)

    wouT = asc(np.asarray(L0_attn_b_W_out))
    nc4 = _get("L4", lambda: build_L2(first=False))
    maps = []
    for c in cores:
        sl = slice(c * TS, (c + 1) * TS)
        maps.append({"x_in": asc(x2[sl]), "xa": asc(outb[sl]), "wT": wouT,
                     "fw": fw, "fwT": fwT, "fb": fb})
    res = run_bass_kernel_spmd(nc4, maps, core_ids=cores)
    x4 = np.concatenate([r["x_out"] for r in res.results], 0)
    return x4.reshape(1, S, D)
